# version 21
# baseline (speedup 1.0000x reference)
import contextlib
import numpy as np
import ml_dtypes
import concourse.bass as bass
import concourse.mybir as mybir
from concourse.bass_utils import run_bass_kernel_spmd
from concourse.alu_op_type import AluOpType

F32 = mybir.dt.float32
BF16 = mybir.dt.bfloat16
AF = mybir.ActivationFunctionType
bft = ml_dtypes.bfloat16

B, L, C = 16, 2048, 1024
NCORES = 8
SEQN = 2
TOK = SEQN * L
TT = 512
NTILE = TOK // TT
NCH = 8
YW = 513
LN1024 = float(np.log(1024.0))
EPSC2 = 1e-5 * 1024.0 * 1024.0


def build_nc():
    nc = bass.Bass("TRN2", target_bir_lowering=False,
                   detect_race_conditions=False)

    xT = nc.dram_tensor("xT", [SEQN * C, L], F32, kind="ExternalInput")
    outT = nc.dram_tensor("outT", [SEQN * C, L], F32, kind="ExternalOutput")
    DR = {}
    for nm in ["wq", "wk", "wvp", "wop", "wrt", "wktf", "wvtf"]:
        DR[nm] = nc.dram_tensor(nm, [C, C], F32, kind="ExternalInput")
    d_bq = nc.dram_tensor("bq", [1, C], F32, kind="ExternalInput")
    d_bk = nc.dram_tensor("bk", [1, C], F32, kind="ExternalInput")
    d_bvp = nc.dram_tensor("bvp", [1, C], F32, kind="ExternalInput")
    d_bo = nc.dram_tensor("bo", [128, 8], F32, kind="ExternalInput")
    d_mur = nc.dram_tensor("mur", [128, 8], F32, kind="ExternalInput")
    d_muk = nc.dram_tensor("muk", [128, 8], F32, kind="ExternalInput")
    d_mask = nc.dram_tensor("maskc", [128, 128], BF16, kind="ExternalInput")
    d_iden = nc.dram_tensor("idenc", [128, 128], BF16, kind="ExternalInput")

    es = contextlib.ExitStack()
    with es:
        w_q = es.enter_context(nc.sbuf_tensor("w_q", [128, 8192], BF16))
        w_k = es.enter_context(nc.sbuf_tensor("w_k", [128, 8192], BF16))
        w_vp = es.enter_context(nc.sbuf_tensor("w_vp", [128, 8192], BF16))
        w_op = es.enter_context(nc.sbuf_tensor("w_op", [128, 8192], BF16))
        w_rt = es.enter_context(nc.sbuf_tensor("w_rt", [128, 8192], BF16))
        w_kt = es.enter_context(nc.sbuf_tensor("w_kt", [128, 8192], BF16))
        w_vt = es.enter_context(nc.sbuf_tensor("w_vt", [128, 8192], BF16))
        R = es.enter_context(nc.sbuf_tensor("R", [128, NCH * TT], F32))
        w2q = es.enter_context(nc.sbuf_tensor("w2q", [2, C], BF16))
        w2k = es.enter_context(nc.sbuf_tensor("w2k", [2, C], BF16))
        f2r = es.enter_context(nc.sbuf_tensor("f2r", [2, C], BF16))
        f2k = es.enter_context(nc.sbuf_tensor("f2k", [2, C], BF16))
        r_bvp = es.enter_context(nc.sbuf_tensor("r_bvp", [1, C], BF16))
        r_csvp = es.enter_context(nc.sbuf_tensor("r_csvp", [1, C], BF16))
        rhs2 = es.enter_context(nc.sbuf_tensor("rhs2", [2, TT], BF16))
        B1d = es.enter_context(nc.sbuf_tensor("B1d", [2, TT], BF16))
        B3d = es.enter_context(nc.sbuf_tensor("B3d", [2, TT], BF16))
        r_B2L = es.enter_context(nc.sbuf_tensor("r_B2L", [1, TT], BF16))
        r_B1f = es.enter_context(nc.sbuf_tensor("r_B1f", [1, YW], BF16))
        r_B3f = es.enter_context(nc.sbuf_tensor("r_B3f", [1, YW], BF16))
        r_csx = r_B2L
        r_s1 = es.enter_context(nc.sbuf_tensor("r_s1", [1, TT], F32))
        r_s2 = es.enter_context(nc.sbuf_tensor("r_s2", [1, TT], F32))
        r_rho = es.enter_context(nc.sbuf_tensor("r_rho", [1, TT], F32))
        r_rhoh = es.enter_context(nc.sbuf_tensor("r_rhoh", [1, TT], BF16))
        r_rhol = es.enter_context(nc.sbuf_tensor("r_rhol", [1, TT], BF16))
        c_pool = es.enter_context(nc.sbuf_tensor("c_pool", [128, 2], F32))
        negC = es.enter_context(nc.sbuf_tensor("negC", [128, 1], BF16))
        onecol = es.enter_context(nc.sbuf_tensor("onecol", [128, 1], BF16))
        mu_r = es.enter_context(nc.sbuf_tensor("mu_r", [128, 8], F32))
        mu_k = es.enter_context(nc.sbuf_tensor("mu_k", [128, 8], F32))
        bo_c = es.enter_context(nc.sbuf_tensor("bo_c", [128, 8], F32))
        murs = es.enter_context(nc.sbuf_tensor("murs", [128, 8], BF16))
        murs2 = es.enter_context(nc.sbuf_tensor("murs2", [128, 8], BF16))
        muks = es.enter_context(nc.sbuf_tensor("muks", [128, 8], BF16))
        muks2 = es.enter_context(nc.sbuf_tensor("muks2", [128, 8], BF16))
        maskb = es.enter_context(nc.sbuf_tensor("maskb", [128, 128], BF16))
        idenb = es.enter_context(nc.sbuf_tensor("idenb", [128, 128], BF16))
        mu_d = es.enter_context(nc.sbuf_tensor("mu_d", [128, 8], F32))
        cy1 = es.enter_context(nc.sbuf_tensor("cy1", [128, 8], F32))
        cy3 = es.enter_context(nc.sbuf_tensor("cy3", [128, 8], F32))
        q_ph = es.enter_context(nc.sbuf_tensor("q_ph", [128, 2048], BF16))
        k_ph = es.enter_context(nc.sbuf_tensor("k_ph", [128, 2048], BF16))
        vg0 = es.enter_context(nc.sbuf_tensor("vg0", [128, 16 * 65], BF16))
        vg1 = es.enter_context(nc.sbuf_tensor("vg1", [128, 16 * 65], BF16))
        vgs = [vg0, vg1]
        ARENA = 41056
        slab = es.enter_context(nc.sbuf_tensor("slab", [128, ARENA],
                                               mybir.dt.uint8))
        ab = nc._sbuf_addr_for_side("left") - ARENA
        def at(name, shape, dtype, off):
            return nc.alloc_sbuf_tensor_at(name, shape, dtype, offset=ab + off)
        xr = at("xr", [128, NCH * TT], BF16, 0)
        xk = at("xk", [128, NCH * TT], BF16, 8192)
        kact = at("kact", [128, NCH * TT], BF16, 16384)
        gsl0 = at("gsl0", [128, YW], F32, 16384)
        gsl1 = at("gsl1", [128, YW], F32, 18464)
        gsl2 = at("gsl2", [128, YW], F32, 20544)
        dmx = at("dmx", [128, 7 * TT], F32, 24576)
        rb = at("rb", [128, 2 * TT], BF16, 28672)
        rsq = at("rsq", [128, 2 * TT], BF16, 30720)
        sg_w = at("sg_w", [128, TT], F32, 32768)
        krl = at("krl", [128, 2 * TT], BF16, 34816)
        ybA = at("ybA", [128, YW], F32, 38912)
        q_fm = at("q_fm", [128, NCH * TT], BF16, 0)
        k_fm = at("k_fm", [128, NCH * TT], BF16, 8192)
        Yb = at("Yb", [128, NCH * TT], BF16, 16384)
        v_tm = at("v_tm", [128, 2048], BF16, 24576)
        expp = at("expp", [128, 2 * TT], BF16, 28672)
        osn = at("osn", [128, 1024], BF16, 30720)
        rzs = at("rzs", [128, 8], F32, 32768)
        rzv = at("rzv", [128, 16], F32, 32800)
        ot = at("ot", [128, 4096], BF16, 32864)
        mmA = es.enter_context(nc.psum_tensor("mmA", [128, TT], F32))
        mmB = es.enter_context(nc.psum_tensor("mmB", [128, TT], F32))
        st0 = es.enter_context(nc.psum_tensor("st0", [128, TT], F32))
        st1 = es.enter_context(nc.psum_tensor("st1", [128, TT], F32))
        pv0 = es.enter_context(nc.psum_tensor("pv0", [128, TT], F32))
        pv1 = es.enter_context(nc.psum_tensor("pv1", [128, TT], F32))
        bcp = es.enter_context(nc.psum_tensor("bcp", [128, TT], F32))
        srw = es.enter_context(nc.psum_tensor("srw", [128, TT], F32))
        SQ = es.enter_context(nc.semaphore("SQ"))
        SG = es.enter_context(nc.semaphore("SG"))
        STN = es.enter_context(nc.semaphore("STn"))
        SV = es.enter_context(nc.semaphore("SV"))
        SS = es.enter_context(nc.semaphore("SS"))
        SP = es.enter_context(nc.semaphore("SP"))
        SR = es.enter_context(nc.semaphore("SR"))
        blk = es.enter_context(nc.Block())
        sem = {"q": SQ, "g": SG, "t": STN, "v": SV, "s": SS, "p": SP,
               "r": SR}
        cnt = {k: 0 for k in sem}
        last = {}

        def mark(instr, name, k=1):
            instr.then_inc(sem[name], k)
            cnt[name] += k
            return cnt[name]

        def wt(eng, ename, name, val):
            if val <= 0:
                return
            key = (ename, name)
            if last.get(key, -1) >= val:
                return
            eng.wait_ge(sem[name], val)
            last[key] = val

        mm = [mmA, mmB]
        stb = [st0, st1]
        pvb = [pv0, pv1]

        def wsl(w, ci, dj):
            return w[:, 1024 * ci + 128 * dj:1024 * ci + 128 * dj + 128]

        @blk.gpsimd
        def _(g):
            W_SB = {"wq": w_q, "wk": w_k, "wvp": w_vp, "wop": w_op,
                    "wrt": w_rt, "wktf": w_kt, "wvtf": w_vt}
            for nm, wsb in W_SB.items():
                for ci in range(8):
                    mark(g.dma_start(
                        wsb[:, 1024 * ci:1024 * ci + 1024],
                        bass.AP(DR[nm], ci * 128 * C, [[C, 128], [1, C]])),
                        "g", 16)
            for dst, src in [(w2q[0:1, :], d_bq.ap()), (w2k[0:1, :], d_bk.ap()),
                             (r_bvp[:], d_bvp.ap())]:
                mark(g.dma_start(dst, src), "g", 16)
            g.memset(rhs2[0:1, :], 1.0)
            g.memset(negC[:], -1.0 / 1024.0)
            g.memset(onecol[:], 1.0)
            g.memset(c_pool[:, 0:1], EPSC2)
            g.memset(c_pool[:, 1:2], LN1024)
            g.memset(vg0[:], 1.0)
            g.memset(vg1[:], 1.0)
            g.memset(r_B1f[:], 0.0)
            g.memset(r_B3f[:], 0.0)
            g.memset(q_ph[:], 0.0)
            mark(g.memset(k_ph[:], 0.0), "p", 1)
        PRO_G = cnt["g"]
        PRO_P = cnt["p"]

        @blk.sync
        def _(q):
            for dst, src in [(mu_r[:], d_mur.ap()), (mu_k[:], d_muk.ap()),
                             (bo_c[:], d_bo.ap()), (maskb[:], d_mask.ap()),
                             (idenb[:], d_iden.ap())]:
                mark(q.dma_start(dst, src), "q", 16)
            for ci in range(NCH):
                mark(q.dma_start(
                    R[:, TT * ci:TT * (ci + 1)],
                    bass.AP(xT, 128 * ci * L, [[L, 128], [1, TT]])), "q", 16)
        PRO_Q = cnt["q"]

        @blk.vector
        def _(v):
            wt(v, "v", "q", PRO_Q)
            wt(v, "v", "g", PRO_G)
            wt(v, "v", "p", PRO_P)
            v.tensor_tensor(mu_d[:], mu_r[:], mu_k[:], op=AluOpType.subtract)
            v.tensor_scalar(murs[:], mu_r[:], -1.0 / 1024.0, None,
                            op0=AluOpType.mult)
            v.tensor_scalar(murs2[:], mu_r[:], 1.0 / 1024.0, -1.0 / 1024.0,
                            op0=AluOpType.mult, op1=AluOpType.add)
            v.tensor_scalar(muks[:], mu_k[:], -1.0 / 1024.0, None,
                            op0=AluOpType.mult)
            mark(v.tensor_scalar(muks2[:], mu_k[:], 1.0 / 1024.0,
                                 -1.0 / 1024.0, op0=AluOpType.mult,
                                 op1=AluOpType.add), "v", 1)
        PRO_V = cnt["v"]

        cs_jobs = [(w2q, 1, w_q, None), (w2k, 1, w_k, None),
                   (r_csvp, 0, w_vp, None),
                   (f2r, 0, w_rt, murs), (f2r, 1, w_rt, murs2),
                   (f2k, 0, w_kt, muks), (f2k, 1, w_kt, muks2)]
        cs_marks = []
        cs_qm = {}
        cs_vm0 = {}
        cs_vm = []

        for half in range(2):
            for ji, (row, part, w, cols) in enumerate(cs_jobs):
                i = half * len(cs_jobs) + ji

                @blk.tensor
                def _(t, i=i, w=w, cols=cols, half=half):
                    if i == 0:
                        wt(t, "t", "g", PRO_G)
                        wt(t, "t", "p", PRO_P)
                        wt(t, "t", "v", PRO_V)
                    else:
                        wt(t, "t", "v", cs_vm0[i - 1])
                    for ci in range(8):
                        col = negC[:, 0:1] if cols is None else cols[:, ci:ci + 1]
                        ins = t.matmul(srw[0:1, :], col,
                                       w[:, 1024 * ci + 512 * half:
                                         1024 * ci + 512 * half + 512],
                                       start=(ci == 0), stop=(ci == 7))
                    cs_marks.append(mark(ins, "t", 1))

                @blk.vector
                def _(v, i=i, row=row, part=part, half=half):
                    wt(v, "v", "t", cs_marks[i])
                    if i >= 1:
                        wt(v, "v", "q", cs_qm.get(i - 1, 0))
                    if part == 0:
                        ins = v.tensor_copy(
                            row[0:1, 512 * half:512 * half + 512],
                            srw[0:1, :])
                    else:
                        ins = v.tensor_copy(r_csx[0:1, :], srw[0:1, :])
                    m = mark(ins, "v", 1)
                    cs_vm.append(m)
                    cs_vm0[i] = m

                @blk.sync
                def _(q, i=i, row=row, part=part, half=half):
                    if part == 1:
                        wt(q, "q", "v", cs_vm[i])
                        mark(q.dma_start(
                            row[1:2, 512 * half:512 * half + 512],
                            r_csx[0:1, :]), "q", 16)
                    cs_qm[i] = cnt["q"]

        CSROWS_V = cnt["v"]
        CSROWS_Q = cnt["q"]

        prev_store_q = 0
        ln_state = {}

        def ln_phase(it, ln_i, ready=None, rsem="p", t_gate=None):
            b_row = {1: B1d, 2: rhs2, 3: B3d}[ln_i]
            carry = ln_i in (1, 3)
            seq_start = (it % 4 == 0)
            rb_marks = []
            rsq_marks = []
            stat_marks = []

            for j in range(NCH):
                sl = slice(TT * (j % 2), TT * (j % 2 + 1))

                @blk.vector
                def _(v, j=j, sl=sl):
                    if j >= 2:
                        wt(v, "v", "t", stat_marks[j - 2])
                    if ready is not None:
                        wt(v, "v", rsem, ready[j])
                    if j == 0:
                        wt(v, "v", "t",
                           cnt["t"] if t_gate is None else t_gate)
                    ins = v.tensor_copy(rb[:, sl], R[:, TT * j:TT * (j + 1)])
                    rb_marks.append(mark(ins, "v", 1))

                @blk.gpsimd
                def _(g, j=j, sl=sl):
                    wt(g, "g", "v", rb_marks[j])
                    if j >= 2:
                        wt(g, "g", "t", stat_marks[j - 2])
                    ins = g.tensor_tensor(rsq[:, sl], rb[:, sl], rb[:, sl],
                                          op=AluOpType.mult)
                    rsq_marks.append(mark(ins, "p", 1))

                @blk.tensor
                def _(t, j=j, sl=sl):
                    wt(t, "t", "v", rb_marks[j])
                    t.matmul(srw[0:1, :], onecol[:], rb[:, sl],
                             start=(j == 0), stop=(j == NCH - 1))
                    wt(t, "t", "p", rsq_marks[j])
                    ins = t.matmul(bcp[32:33, :], onecol[:], rsq[:, sl],
                                   start=(j == 0), stop=(j == NCH - 1))
                    stat_marks.append(mark(ins, "t", 1))

            @blk.vector
            def _(v):
                wt(v, "v", "t", stat_marks[NCH - 1])
                v.tensor_copy(r_s1[:], srw[0:1, :])
                v.tensor_tensor(r_rho[:], r_s1[:], r_s1[:], op=AluOpType.mult)
                mark(v.scalar_tensor_tensor(r_s2[:], bcp[32:33, :], 1024.0,
                                            r_rho[:], op0=AluOpType.mult,
                                            op1=AluOpType.subtract), "v", 1)
            v_u = cnt["v"]

            @blk.scalar
            def _(s):
                wt(s, "s", "v", v_u)
                s.activation(r_s2[:], r_s2[:], AF.Ln, bias=c_pool[0:1, 0:1])
                s.activation(r_rho[:], r_s2[:], AF.Exp, scale=-0.5,
                             bias=c_pool[0:1, 1:2])
                mark(s.activation(r_rhoh[:], r_rho[:], AF.Copy,
                                  bias=0.0), "s", 1)
            s_rho = cnt["s"]

            @blk.vector
            def _(v):
                wt(v, "v", "s", s_rho)
                mark(v.scalar_tensor_tensor(r_rhol[:], r_rhoh[:], -1.0,
                                            r_rho[:], op0=AluOpType.mult,
                                            op1=AluOpType.add), "v", 1)
            v_rows = cnt["v"]

            @blk.vector
            def _(v):
                if carry:
                    bf_row = r_B1f if ln_i == 1 else r_B3f
                    v.tensor_copy(bf_row[0:1, 0:1], bf_row[0:1, TT:TT + 1])
                    v.drain()
                    ins = v.tensor_tensor(bf_row[0:1, 1:TT + 1], r_s1[:],
                                          r_rho[:], op=AluOpType.mult)
                    if seq_start:
                        ins = v.memset(bf_row[0:1, 0:1], 0.0)
                else:
                    ins = v.tensor_tensor(r_B2L[0:1, 0:TT], r_s1[:],
                                          r_rho[:], op=AluOpType.mult)
                mark(ins, "v", 1)
            v_brow = cnt["v"]

            @blk.sync
            def _(q):
                wt(q, "q", "v", v_brow)
                if carry:
                    bf_row = r_B1f if ln_i == 1 else r_B3f
                    mark(q.dma_start(b_row[0:1, 0:TT],
                                     bf_row[0:1, 1:TT + 1]), "q", 16)
                    mark(q.dma_start(b_row[1:2, 0:TT],
                                     bf_row[0:1, 0:TT]), "q", 16)
                else:
                    mark(q.dma_start(rhs2[1:2, 0:TT],
                                     r_B2L[0:1, 0:TT]), "q", 16)
            ln_state[(it, ln_i, "bq")] = cnt["q"]

            @blk.tensor
            def _(t):
                wt(t, "t", "s", s_rho)
                t.matmul(bcp[:], rhs2[0:1, 0:128], r_rhoh[:],
                         start=True, stop=False)
                wt(t, "t", "v", v_rows)
                ins = t.matmul(bcp[:], rhs2[0:1, 0:128], r_rhol[:],
                               start=False, stop=True)
                mark(ins, "t", 1)
            t_bc = cnt["t"]

            if ln_i == 2:
                @blk.vector
                def _(v):
                    wt(v, "v", "t", max(t_bc, cnt["t"]))
                    for j in range(NCH):
                        ins = v.tensor_tensor(Yb[:, TT * j:TT * (j + 1)],
                                              R[:, TT * j:TT * (j + 1)],
                                              bcp[:], op=AluOpType.mult)
                        ln_state[(it, 2, "yb", j)] = mark(ins, "v", 1)
            ln_state[(it, ln_i)] = cnt["v"]
            ln_state[(it, ln_i, "bc")] = t_bc
            return b_row


        def ffn_emit(it, b_row, bq_mark, mix_k, mix_r):
            bpair = b_row[0:2, 0:TT]
            kev = {}
            kevp = {}
            fev = {}
            fsg = {}
            grm = {}
            kmm_marks = {}

            s_now = cnt["s"]
            for dj in range(NCH):
                bank = mm[dj % 2]

                @blk.tensor
                def _(t, dj=dj, bank=bank):
                    if dj < 2:
                        wt(t, "t", "s", s_now)
                        wt(t, "t", "q", bq_mark)
                    if dj >= 2:
                        wt(t, "t", "v", kev[dj - 2])
                    for oi, ci in enumerate(MIX_ORDER):
                        sm, val = mix_k[ci]
                        wt(t, "t", sm, val)
                        t.matmul(bank[:], wsl(w_kt, ci, dj),
                                 xk[:, TT * ci:TT * (ci + 1)],
                                 start=(oi == 0), stop=False)
                    ins = t.matmul(bank[:],
                                   f2k[0:2, 128 * dj:128 * dj + 128],
                                   bpair, start=False, stop=True)
                    kmm_marks[dj] = mark(ins, "t", 1)

                @blk.vector
                def _(v, dj=dj, bank=bank):
                    wt(v, "v", "t", kmm_marks[dj])
                    wt(v, "v", "p", kevp.get(dj - 2, 0))
                    sl = krl[:, TT * (dj % 2):TT * (dj % 2 + 1)]
                    ins = v.tensor_scalar(sl, bank[:], 0.0, None,
                                          op0=AluOpType.max)
                    kev[dj] = mark(ins, "v", 1)

                @blk.gpsimd
                def _(g, dj=dj):
                    wt(g, "g", "v", kev[dj])
                    sl = krl[:, TT * (dj % 2):TT * (dj % 2 + 1)]
                    ins = g.tensor_tensor(kact[:, TT * dj:TT * (dj + 1)],
                                          sl, sl, op=AluOpType.mult)
                    kevp[dj] = mark(ins, "p", 1)

            rpre_m = {}
            vpre_m = {}
            for dj in range(NCH):
                bankR = mm[dj % 2]
                bankV = stb[dj % 2]

                @blk.tensor
                def _(t, dj=dj, bankR=bankR, bankV=bankV):
                    if dj < 2:
                        wt(t, "t", "s", s_now)
                    if dj >= 2:
                        wt(t, "t", "s", fsg[dj - 2])
                        wt(t, "t", "v", fev[dj - 2])
                    for oi, ci in enumerate(MIX_ORDER):
                        sm, val = mix_r[ci]
                        wt(t, "t", sm, val)
                        t.matmul(bankR[:], wsl(w_rt, ci, dj),
                                 xr[:, TT * ci:TT * (ci + 1)],
                                 start=(oi == 0), stop=False)
                    ins = t.matmul(bankR[:],
                                   f2r[0:2, 128 * dj:128 * dj + 128],
                                   bpair, start=False, stop=True)
                    rpre_m[dj] = mark(ins, "t", 1)
                    for ci in range(NCH):
                        wt(t, "t", "p", kevp[ci])
                        ins = t.matmul(bankV[:], wsl(w_vt, ci, dj),
                                       kact[:, TT * ci:TT * (ci + 1)],
                                       start=(ci == 0), stop=(ci == NCH - 1))
                    vpre_m[dj] = mark(ins, "t", 1)

                @blk.scalar
                def _(s, dj=dj, bankR=bankR):
                    wt(s, "s", "t", rpre_m[dj])
                    wt(s, "s", "v", fev.get(dj - 1, 0))
                    ins = s.activation(sg_w[:], bankR[:], AF.Sigmoid)
                    fsg[dj] = mark(ins, "s", 1)

                @blk.vector
                def _(v, dj=dj, bankV=bankV):
                    wt(v, "v", "s", fsg[dj])
                    wt(v, "v", "t", vpre_m[dj])
                    wt(v, "v", "p", grm.get(dj - 1, 0))
                    ins = v.scalar_tensor_tensor(sg_w[:], sg_w[:], 0.5,
                                                 bankV[:],
                                                 op0=AluOpType.mult,
                                                 op1=AluOpType.mult)
                    fev[dj] = mark(ins, "v", 1)

                @blk.gpsimd
                def _(g, dj=dj):
                    wt(g, "g", "v", fev[dj])
                    ins = g.tensor_tensor(R[:, TT * dj:TT * (dj + 1)],
                                           R[:, TT * dj:TT * (dj + 1)],
                                           sg_w[:], op=AluOpType.add)
                    grm[dj] = mark(ins, "p", 1)
            return grm

        MIX_ORDER = list(range(NCH))

        def mix_emit(it, ln_i, b_row):
            mix_k, mix_r = {}, {}
            y_ready = ln_state[(it, ln_i)]
            t_bc = ln_state[(it, ln_i, "bc")]
            seq_start = (it % 4 == 0)
            cyb = cy1 if ln_i == 1 else cy3
            slots = [gsl0, gsl1, gsl2, ybA]

            slot_m = {}
            d_m = {}

            def S(j):
                slot = slots[j % 4]

                @blk.vector
                def _(v, j=j, slot=slot):
                    if j == 0:
                        wt(v, "v", "v", y_ready)
                        wt(v, "v", "t", max(t_bc, cnt["t"]))
                    if seq_start:
                        v.memset(slot[:, 0:1], 0.0)
                    else:
                        v.tensor_copy(slot[:, 0:1], cyb[:, j:j + 1])
                    v.tensor_tensor(slot[:, 1:YW], R[:, TT * j:TT * (j + 1)],
                                    bcp[:], op=AluOpType.mult)
                    v.drain()
                    ins = v.tensor_copy(cyb[:, j:j + 1], slot[:, TT:TT + 1])
                    slot_m[j] = mark(ins, "v", 1)

                @blk.gpsimd
                def _(g, j=j, slot=slot):
                    wt(g, "g", "v", slot_m[j])
                    dsl = dmx[:, TT * (j % 7):TT * (j % 7 + 1)]
                    ins = g.tensor_tensor(dsl, slot[:, 1:YW],
                                          slot[:, 0:TT], op=AluOpType.subtract)
                    d_m[j] = mark(ins, "p", 1)

            def XK(j):
                slot = slots[j % 4]

                @blk.vector
                def _(v, j=j, slot=slot):
                    wt(v, "v", "p", d_m[j])
                    dsl = dmx[:, TT * (j % 7):TT * (j % 7 + 1)]
                    ins = v.scalar_tensor_tensor(
                        xk[:, TT * j:TT * (j + 1)], dsl,
                        mu_k[:, j:j + 1], slot[:, 0:TT],
                        op0=AluOpType.mult, op1=AluOpType.add)
                    mix_k[j] = ("v", mark(ins, "v", 1))

            def XR(j):
                @blk.vector
                def _(v, j=j):
                    dsl = dmx[:, TT * (j % 7):TT * (j % 7 + 1)]
                    ins = v.scalar_tensor_tensor(
                        xr[:, TT * j:TT * (j + 1)], dsl,
                        mu_d[:, j:j + 1], xk[:, TT * j:TT * (j + 1)],
                        op0=AluOpType.mult, op1=AluOpType.add)
                    mix_r[j] = ("v", mark(ins, "v", 1))

            S(0)
            S(1)
            for j in range(NCH):
                XK(j)
                if j == 5:
                    XR(0)
                if j + 2 < NCH:
                    S(j + 2)
            for j in range(1, NCH):
                XR(j)

            return mix_k, mix_r

        def attn_emit(it):
            prev = attn_prev
            qk_ev = {}
            v_now0 = cnt["v"]
            s_now0 = cnt["s"]
            jobs = [(w_q, q_fm, w2q, dj) for dj in range(NCH)] + \
                   [(w_k, k_fm, w2k, dj) for dj in range(NCH)]
            for i, (wmat, dstb, pair, dj) in enumerate(jobs):
                bank = mm[i % 2]

                @blk.tensor
                def _(t, i=i, wmat=wmat, dj=dj, bank=bank, pair=pair):
                    if i < 2:
                        wt(t, "t", "s", s_now0)
                        wt(t, "t", "q", ln_state[(it, 2, "bq")])
                    else:
                        wt(t, "t", "v", qk_ev[i - 2])
                    for ci in range(NCH):
                        wt(t, "t", "v", ln_state[(it, 2, "yb", ci)])
                        t.matmul(bank[:], wsl(wmat, ci, dj),
                                 Yb[:, TT * ci:TT * (ci + 1)],
                                 start=(ci == 0), stop=False)
                    ins = t.matmul(bank[:],
                                   pair[0:2, 128 * dj:128 * dj + 128],
                                   rhs2[0:2, 0:TT], start=False, stop=True)
                    qk_ev[(i, "t")] = mark(ins, "t", 1)

                @blk.vector
                def _(v, i=i, dj=dj, bank=bank, dstb=dstb):
                    wt(v, "v", "t", qk_ev[(i, "t")])
                    dst = bass.AP(dstb, 8 * dj,
                                  [[NCH * TT, 128], [1024, 4], [1, 8],
                                   [64, 16]])
                    srcp = bass.AP(bank, 0,
                                   [[TT, 128], [128, 4], [16, 8], [1, 16]])
                    ins = v.tensor_copy(dst, srcp)
                    qk_ev[i] = mark(ins, "v", 1)

            qk_done = cnt["v"]

            sub_state = {}

            vt_ms = {}

            def vpathA(s_i):
                vt_m = vt_ms.setdefault(s_i, {})
                vsl = v_tm[:, 1024 * (s_i % 2):1024 * (s_i % 2) + 1024]
                for nh in range(2):
                    bank = mm[nh]

                    @blk.tensor
                    def _(t, s_i=s_i, nh=nh, bank=bank):
                        wt(t, "t", "v", qk_done)
                        if s_i > 0:
                            wt(t, "t", "v", sub_state[(s_i - 1, "vtm_ev")])
                        for ci in range(NCH):
                            wt(t, "t", "v", ln_state[(it, 2, "yb", ci)])
                            t.matmul(bank[:],
                                     Yb[:, TT * ci + 128 * s_i:
                                        TT * ci + 128 * s_i + 128],
                                     w_vp[:, 1024 * ci + 512 * nh:
                                          1024 * ci + 512 * nh + 512],
                                     start=(ci == 0), stop=False)
                        t.matmul(bank[:], rhs2[0:1, 0:128],
                                 r_bvp[0:1, 512 * nh:512 * nh + 512],
                                 start=False, stop=False)
                        ins = t.matmul(
                            bank[:], r_B2L[0:1, 128 * s_i:128 * s_i + 128],
                            r_csvp[0:1, 512 * nh:512 * nh + 512],
                            start=False, stop=True)
                        vt_m[nh] = mark(ins, "t", 1)

            def vpathAe(s_i):
                vt_m = vt_ms[s_i]
                vsl = v_tm[:, 1024 * (s_i % 2):1024 * (s_i % 2) + 1024]

                @blk.vector
                def _(v, s_i=s_i, vsl=vsl, vt_m=vt_m):
                    wt(v, "v", "t", vt_m[1])
                    pk = (s_i - 2, "vgd_r")
                    if pk in sub_state:
                        wt(v, "v", "r", sub_state[pk])
                        wt(v, "v", "q", sub_state[(s_i - 2, "vgd_q")])
                    else:
                        pr = prev.get("vgd%d" % (s_i % 2))
                        if pr is not None:
                            wt(v, "v", "r", pr[0])
                            wt(v, "v", "q", pr[1])
                    v.tensor_copy(vsl[:, 0:512], mm[0][:])
                    ins = v.tensor_copy(vsl[:, 512:1024], mm[1][:])
                    sub_state[(s_i, "vtm_ev")] = mark(ins, "v", 1)

            def vpathB(s_i):
                vsl = v_tm[:, 1024 * (s_i % 2):1024 * (s_i % 2) + 1024]
                vgb = vgs[s_i % 2]

                @blk.scalar
                def _(s, s_i=s_i, vsl=vsl, vgb=vgb):
                    wt(s, "s", "v", sub_state[(s_i, "vtm_ev")])
                    wt(s, "s", "t", sub_state.get((s_i - 2, "pv_all"),
                                                  prev.get("pv%d" % (s_i % 2), 0)))
                    for sg in range(0, 16, 2):
                        srcap = bass.AP(vsl.tensor if hasattr(vsl, "tensor")
                                        else v_tm,
                                        1024 * (s_i % 2) + 64 * sg,
                                        [[2048, 128], [1, 64]])
                        dstap = bass.AP(vgb, 8 * sg * (16 * 65),
                                        [[16 * 65, 8], [65, 16], [1, 64]])
                        mark(s.dma_start(dstap, srcap), "r", 16)
                    sub_state[(s_i, "vgd_r")] = cnt["r"]

                @blk.sync
                def _(q, s_i=s_i, vsl=vsl, vgb=vgb):
                    wt(q, "q", "v", sub_state[(s_i, "vtm_ev")])
                    wt(q, "q", "t", sub_state.get((s_i - 2, "pv_all"),
                                                  prev.get("pv%d" % (s_i % 2), 0)))
                    for sg in range(1, 16, 2):
                        srcap = bass.AP(vsl.tensor if hasattr(vsl, "tensor")
                                        else v_tm,
                                        1024 * (s_i % 2) + 64 * sg,
                                        [[2048, 128], [1, 64]])
                        dstap = bass.AP(vgb, 8 * sg * (16 * 65),
                                        [[16 * 65, 8], [65, 16], [1, 64]])
                        mark(q.dma_start(dstap, srcap), "q", 16)
                    sub_state[(s_i, "vgd_q")] = cnt["q"]

            def parity(s_i):
                @blk.sync
                def _(q, s_i=s_i):
                    if s_i > 0:
                        wt(q, "q", "t", sub_state[(s_i - 1, "sc_all")])
                    else:
                        wt(q, "q", "t", prev.get("sc_all", 0))
                    wt(q, "q", "v", qk_done)
                    for par in range(2):
                        for srcb, dstb in [(q_fm, q_ph), (k_fm, k_ph)]:
                            srcap = bass.AP(srcb,
                                            64 * par * NCH * TT + 1024 * s_i,
                                            [[NCH * TT, 64], [64, 16], [1, 64]])
                            dstap = bass.AP(dstb, 64 * par,
                                            [[2048, 64], [128, 16], [1, 64]])
                            mark(q.dma_start(dstap, srcap), "q", 16)
                    sub_state[(s_i, "phd")] = cnt["q"]

            vpathA(0)
            vpathAe(0)
            parity(0)
            vpathB(0)
            for s_i in range(4):
                vgb = vgs[s_i % 2]
                if s_i < 3:
                    vpathA(s_i + 1)

                sc_m = {}
                ex_m = {}
                pv_m = {}

                def emit_sc(bnk, s_i=s_i):
                    bank = stb[bnk % 2]

                    @blk.tensor
                    def _(t, s_i=s_i, bnk=bnk, bank=bank):
                        wt(t, "t", "q", sub_state[(s_i, "phd")])
                        if bnk >= 2:
                            wt(t, "t", "s", ex_m[bnk - 2])
                        elif s_i > 0:
                            wt(t, "t", "s", sub_state[(s_i - 1, "ex", bnk + 2)])
                        for gl in range(4):
                            g = 4 * bnk + gl
                            t.matmul(bank[:, 128 * gl:128 * gl + 128],
                                     k_ph[0:64, 128 * g:128 * g + 128],
                                     q_ph[0:64, 128 * g:128 * g + 128],
                                     start=True, stop=False)
                            ins = t.matmul(bank[:, 128 * gl:128 * gl + 128],
                                           idenb[:], maskb[:],
                                           start=False, stop=True)
                        sc_m[bnk] = mark(ins, "t", 1)

                def emit_exp(bnk, s_i=s_i):
                    bank = stb[bnk % 2]

                    @blk.scalar
                    def _(s, s_i=s_i, bnk=bnk, bank=bank):
                        wt(s, "s", "t", sc_m[bnk])
                        if bnk >= 2:
                            wt(s, "s", "t", pv_m[bnk - 2])
                        elif s_i > 0:
                            wt(s, "s", "t", sub_state[(s_i - 1, "pv", bnk + 2)])
                        ins = s.activation(
                            expp[:, TT * (bnk % 2):TT * (bnk % 2 + 1)],
                            bank[:], AF.Exp, scale=0.125)
                        ex_m[bnk] = mark(ins, "s", 1)

                def emit_pv(bnk, s_i=s_i, vgb=vgb):
                    @blk.tensor
                    def _(t, s_i=s_i, bnk=bnk, vgb=vgb):
                        wt(t, "t", "s", ex_m[bnk])
                        wt(t, "t", "r", sub_state[(s_i, "vgd_r")])
                        wt(t, "t", "q", sub_state[(s_i, "vgd_q")])
                        pbank = pvb[bnk % 2]
                        if bnk >= 2:
                            wt(t, "t", "v", pv_m.get((bnk - 2, "rz"), 0))
                        elif s_i > 0:
                            wt(t, "t", "v",
                               sub_state[(s_i - 1, "rz", bnk + 2)])
                        for gl in range(4):
                            g = 4 * bnk + gl
                            ins = t.matmul(pbank[:, 65 * gl:65 * gl + 65],
                                           expp[:, TT * (bnk % 2) + 128 * gl:
                                                TT * (bnk % 2) + 128 * gl + 128],
                                           vgb[:, 65 * g:65 * g + 65],
                                           start=True, stop=True)
                        pv_m[bnk] = mark(ins, "t", 1)

                    @blk.vector
                    def _(v, s_i=s_i, bnk=bnk):
                        wt(v, "v", "t", pv_m[bnk])
                        pbank = pvb[bnk % 2]
                        v.tensor_copy(rzs[:, 0:4],
                                      bass.AP(pbank, 64, [[TT, 128], [65, 4]]))
                        v.drain()
                        v.reciprocal(rzv[:, 4 * bnk:4 * bnk + 4], rzs[:, 0:4])
                        v.drain()
                        rzb = bass.AP(rzv, 4 * bnk,
                                      [[16, 128], [1, 4], [0, 64]])
                        ins = v.tensor_tensor(
                            bass.AP(osn, 256 * bnk, [[1024, 128], [1, 256]]),
                            bass.AP(pbank, 0, [[TT, 128], [65, 4], [1, 64]]),
                            rzb, op=AluOpType.mult)
                        pv_m[(bnk, "rz")] = mark(ins, "v", 1)

                emit_sc(0)
                emit_exp(0)
                emit_sc(1)
                emit_pv(0)
                emit_exp(1)
                emit_sc(2)
                emit_pv(1)
                emit_exp(2)
                emit_sc(3)
                emit_pv(2)
                emit_exp(3)
                emit_pv(3)
                sub_state[(s_i, "sc_all")] = sc_m[3]
                if s_i < 3:
                    vpathAe(s_i + 1)
                    parity(s_i + 1)
                    vpathB(s_i + 1)

                @blk.vector
                def _(v, s_i=s_i):
                    wt(v, "v", "v", pv_m[(3, "rz")])
                    v.drain()
                    ins = v.transpose(ot[:, 1024 * s_i:1024 * s_i + 1024],
                                      osn[:])
                    sub_state[(s_i, "ot")] = mark(ins, "v", 1)
                sub_state[(s_i, "pv_all")] = pv_m[3]
                for bnk in range(4):
                    sub_state[(s_i, "ex", bnk)] = ex_m[bnk]
                    sub_state[(s_i, "pv", bnk)] = pv_m[bnk]
                    sub_state[(s_i, "rz", bnk)] = pv_m[(bnk, "rz")]

            wo_m = {}
            grm = {}
            for dj in range(NCH):
                bank = mm[dj % 2]

                @blk.tensor
                def _(t, dj=dj, bank=bank):
                    wt(t, "t", "v", sub_state[(3, "ot")])
                    if dj >= 2:
                        wt(t, "t", "v", wo_m.get((dj - 2, "ev"), 0))
                    for f in range(8):
                        off = 32 * (f // 4) + 8 * (f % 4)
                        rhs = bass.AP(ot, off, [[4096, 128], [64, 64], [1, 8]])
                        ins = t.matmul(bank[:], wsl(w_op, f, dj), rhs,
                                       start=(f == 0), stop=(f == 7))
                    wo_m[dj] = mark(ins, "t", 1)

                @blk.vector
                def _(v, dj=dj, bank=bank):
                    wt(v, "v", "t", wo_m[dj])
                    for s_i in range(4):
                        dst = bass.AP(R, TT * dj + 128 * s_i,
                                      [[NCH * TT, 128], [1, 16], [16, 8]])
                        src = bass.AP(bank, 128 * s_i,
                                      [[TT, 128], [8, 16], [1, 8]])
                        ins = v.scalar_tensor_tensor(
                            dst, src, bo_c[:, dj:dj + 1], dst,
                            op0=AluOpType.add, op1=AluOpType.add)
                    wo_m[(dj, "ev")] = mark(ins, "v", 1)
            attn_prev.clear()
            attn_prev.update({"vgd0": (sub_state[(2, "vgd_r")],
                                       sub_state[(2, "vgd_q")]),
                              "vgd1": (sub_state[(3, "vgd_r")],
                                       sub_state[(3, "vgd_q")]),
                              "sc_all": sub_state[(3, "sc_all")],
                              "pv0": sub_state[(2, "pv_all")],
                              "pv1": sub_state[(3, "pv_all")]})
            return {dj: wo_m[(dj, "ev")] for dj in range(NCH)}

        store_marks = []
        attn_prev = {}
        prev_attn_t = [0]
        for it in range(NTILE):
            seq, t0 = it // 4, (it % 4) * TT

            xl_marks = []
            if it > 0:
                @blk.sync
                def _(q, it=it, seq=seq, t0=t0):
                    for ci in range(NCH):
                        wt(q, "q", "q", store_marks[-1][ci])
                        mark(q.dma_start(
                            R[:, TT * ci:TT * (ci + 1)],
                            bass.AP(xT, seq * C * L + 128 * ci * L + t0,
                                    [[L, 128], [1, TT]])), "q", 16)
                        xl_marks.append(cnt["q"])
            else:
                xl_marks = [PRO_Q] * NCH

            if it == 0:
                @blk.vector
                def _(v):
                    wt(v, "v", "v", CSROWS_V)
                    wt(v, "v", "q", PRO_Q)
                    mark(v.nop(), "v", 1)

            ln_phase(it, 1, ready=xl_marks, rsem="q",
                     t_gate=prev_attn_t[0] if it > 0 else None)
            mk1, mr1 = mix_emit(it, 1, B1d)
            grm1 = ffn_emit(it, B1d, ln_state[(it, 1, 'bq')], mk1, mr1)

            ln_phase(it, 2, ready=grm1)
            wo_ev = attn_emit(it)
            prev_attn_t[0] = cnt["t"]

            ln_phase(it, 3, t_gate=prev_attn_t[0])
            mk3, mr3 = mix_emit(it, 3, B3d)
            grm = ffn_emit(it, B3d, ln_state[(it, 3, 'bq')], mk3, mr3)

            @blk.sync
            def _(q, it=it, seq=seq, t0=t0):
                sm = []
                for ci in range(NCH):
                    wt(q, "q", "p", grm[ci])
                    mark(q.dma_start(
                        bass.AP(outT, seq * C * L + 128 * ci * L + t0,
                                [[L, 128], [1, TT]]),
                        R[:, TT * ci:TT * (ci + 1)]), "q", 16)
                    sm.append(cnt["q"])
                store_marks.append(sm)

        @blk.sync
        def _(q):
            wt(q, "q", "q", cnt["q"])

        @blk.gpsimd
        def _(g):
            wt(g, "g", "q", cnt["q"])

    return nc


_sigma_head = [2 * (s % 8) + s // 8 for s in range(16)]
_PERMV = np.array([64 * _sigma_head[s] + d for s in range(16)
                   for d in range(64)])


def _wo_perm(Wo):
    WoP = np.zeros_like(Wo)
    out = np.zeros((8, 128, 1024), dtype=Wo.dtype)
    for f in range(8):
        dhi, sm4 = f // 4, f % 4
        for p in range(128):
            sq, dlo = p // 32, p % 32
            sigma = 4 * sq + sm4
            h = 2 * (sigma % 8) + sigma // 8
            d = 32 * dhi + dlo
            out[f, p, :] = Wo[64 * h + d, :]
    return out.reshape(8 * 128, 1024)


_NC_CACHE = {}


def kernel(L0_attn_x, L0_attn_Wq, L0_attn_bq, L0_attn_Wk, L0_attn_bk,
           L0_attn_Wv, L0_attn_bv, L0_attn_Wo, L0_attn_bo,
           L0_ffn_W_r, L0_ffn_W_k, L0_ffn_W_v, L0_ffn_mu_r, L0_ffn_mu_k):
    x = np.ascontiguousarray(np.asarray(L0_attn_x, dtype=np.float32))
    Wq = np.asarray(L0_attn_Wq, np.float32)
    Wk = np.asarray(L0_attn_Wk, np.float32)
    Wv = np.asarray(L0_attn_Wv, np.float32)
    Wo = np.asarray(L0_attn_Wo, np.float32)
    Wr_f = np.asarray(L0_ffn_W_r, np.float32)
    Wk_f = np.asarray(L0_ffn_W_k, np.float32)
    Wv_f = np.asarray(L0_ffn_W_v, np.float32)

    mask = np.full((128, 128), -30000.0, dtype=np.float32)
    for te in range(8):
        for se in range(16):
            for sh in range(16):
                mask[8 * se + te, 8 * sh + te] = 0.0
    iden = np.eye(128, dtype=np.float32)

    common = {
        "wq": np.ascontiguousarray(Wq),
        "wk": np.ascontiguousarray(Wk),
        "wvp": np.ascontiguousarray(Wv[:, _PERMV]),
        "wop": np.ascontiguousarray(_wo_perm(Wo)),
        "wrt": np.ascontiguousarray(Wr_f.T),
        "wktf": np.ascontiguousarray(Wk_f.T),
        "wvtf": np.ascontiguousarray(Wv_f.T),
        "bq": np.asarray(L0_attn_bq, np.float32).reshape(1, C),
        "bk": np.asarray(L0_attn_bk, np.float32).reshape(1, C),
        "bvp": np.asarray(L0_attn_bv, np.float32)[_PERMV].reshape(1, C),
        "bo": np.ascontiguousarray(
            np.asarray(L0_attn_bo, np.float32).reshape(8, 128).T),
        "mur": np.ascontiguousarray(
            np.asarray(L0_ffn_mu_r, np.float32).reshape(8, 128).T),
        "muk": np.ascontiguousarray(
            np.asarray(L0_ffn_mu_k, np.float32).reshape(8, 128).T),
        "maskc": mask.astype(bft),
        "idenc": iden.astype(bft),
    }

    xT = np.ascontiguousarray(x.transpose(0, 2, 1))
    in_maps = []
    for core in range(NCORES):
        m = dict(common)
        m["xT"] = np.ascontiguousarray(
            xT[SEQN * core:SEQN * (core + 1)].reshape(SEQN * C, L))
        in_maps.append(m)

    if "nc" not in _NC_CACHE:
        _NC_CACHE["nc"] = build_nc()
    res = run_bass_kernel_spmd(_NC_CACHE["nc"], in_maps,
                               core_ids=list(range(NCORES)))
    out = np.empty((B, L, C), dtype=np.float32)
    for core in range(NCORES):
        o = res.results[core]["outT"].reshape(SEQN, C, L)
        out[SEQN * core:SEQN * (core + 1)] = o.transpose(0, 2, 1)
    return out


# version 22 (final)
# speedup vs baseline: 1.0644x; 1.0644x over previous
import contextlib
import numpy as np
import ml_dtypes
import concourse.bass as bass
import concourse.mybir as mybir
from concourse.bass_utils import run_bass_kernel_spmd
from concourse.alu_op_type import AluOpType

F32 = mybir.dt.float32
BF16 = mybir.dt.bfloat16
AF = mybir.ActivationFunctionType
bft = ml_dtypes.bfloat16

B, L, C = 16, 2048, 1024
NCORES = 8
SEQN = 2
TOK = SEQN * L
TT = 512
NTILE = TOK // TT
NCH = 8
YW = 513
LN1024 = float(np.log(1024.0))
EPSC2 = 1e-5 * 1024.0 * 1024.0


def build_nc():
    nc = bass.Bass("TRN2", target_bir_lowering=False,
                   detect_race_conditions=False)

    xT = nc.dram_tensor("xT", [SEQN * C, L], F32, kind="ExternalInput")
    outT = nc.dram_tensor("outT", [SEQN * C, L], F32, kind="ExternalOutput")
    DR = {}
    for nm in ["wq", "wk", "wvp", "wop", "wrt", "wktf", "wvtf"]:
        DR[nm] = nc.dram_tensor(nm, [C, C], F32, kind="ExternalInput")
    d_bq = nc.dram_tensor("bq", [1, C], F32, kind="ExternalInput")
    d_bk = nc.dram_tensor("bk", [1, C], F32, kind="ExternalInput")
    d_bvp = nc.dram_tensor("bvp", [1, C], F32, kind="ExternalInput")
    d_bo = nc.dram_tensor("bo", [128, 8], F32, kind="ExternalInput")
    d_mur = nc.dram_tensor("mur", [128, 8], F32, kind="ExternalInput")
    d_muk = nc.dram_tensor("muk", [128, 8], F32, kind="ExternalInput")
    d_mask = nc.dram_tensor("maskc", [128, 128], BF16, kind="ExternalInput")
    d_iden = nc.dram_tensor("idenc", [128, 128], BF16, kind="ExternalInput")

    es = contextlib.ExitStack()
    with es:
        w_q = es.enter_context(nc.sbuf_tensor("w_q", [128, 8192], BF16))
        w_k = es.enter_context(nc.sbuf_tensor("w_k", [128, 8192], BF16))
        w_vp = es.enter_context(nc.sbuf_tensor("w_vp", [128, 8192], BF16))
        w_op = es.enter_context(nc.sbuf_tensor("w_op", [128, 8192], BF16))
        w_rt = es.enter_context(nc.sbuf_tensor("w_rt", [128, 8192], BF16))
        w_kt = es.enter_context(nc.sbuf_tensor("w_kt", [128, 8192], BF16))
        w_vt = es.enter_context(nc.sbuf_tensor("w_vt", [128, 8192], BF16))
        R = es.enter_context(nc.sbuf_tensor("R", [128, NCH * TT], F32))
        w2q = es.enter_context(nc.sbuf_tensor("w2q", [2, C], BF16))
        w2k = es.enter_context(nc.sbuf_tensor("w2k", [2, C], BF16))
        f2r = es.enter_context(nc.sbuf_tensor("f2r", [2, C], BF16))
        f2k = es.enter_context(nc.sbuf_tensor("f2k", [2, C], BF16))
        r_bvp = es.enter_context(nc.sbuf_tensor("r_bvp", [1, C], BF16))
        r_csvp = es.enter_context(nc.sbuf_tensor("r_csvp", [1, C], BF16))
        rhs2 = es.enter_context(nc.sbuf_tensor("rhs2", [2, TT], BF16))
        B1d = es.enter_context(nc.sbuf_tensor("B1d", [2, TT], BF16))
        B3d = es.enter_context(nc.sbuf_tensor("B3d", [2, TT], BF16))
        r_B2L = es.enter_context(nc.sbuf_tensor("r_B2L", [1, TT], BF16))
        r_B1f = es.enter_context(nc.sbuf_tensor("r_B1f", [1, YW], BF16))
        r_B3f = es.enter_context(nc.sbuf_tensor("r_B3f", [1, YW], BF16))
        r_csx = r_B2L
        r_s1 = es.enter_context(nc.sbuf_tensor("r_s1", [1, TT], F32))
        r_s2 = es.enter_context(nc.sbuf_tensor("r_s2", [1, TT], F32))
        r_rho = es.enter_context(nc.sbuf_tensor("r_rho", [1, TT], F32))
        r_rhoh = es.enter_context(nc.sbuf_tensor("r_rhoh", [1, TT], BF16))
        r_rhol = es.enter_context(nc.sbuf_tensor("r_rhol", [1, TT], BF16))
        c_pool = es.enter_context(nc.sbuf_tensor("c_pool", [128, 2], F32))
        negC = es.enter_context(nc.sbuf_tensor("negC", [128, 1], BF16))
        onecol = es.enter_context(nc.sbuf_tensor("onecol", [128, 1], BF16))
        mu_r = es.enter_context(nc.sbuf_tensor("mu_r", [128, 8], F32))
        mu_k = es.enter_context(nc.sbuf_tensor("mu_k", [128, 8], F32))
        bo_c = es.enter_context(nc.sbuf_tensor("bo_c", [128, 8], F32))
        murs = es.enter_context(nc.sbuf_tensor("murs", [128, 8], BF16))
        murs2 = es.enter_context(nc.sbuf_tensor("murs2", [128, 8], BF16))
        muks = es.enter_context(nc.sbuf_tensor("muks", [128, 8], BF16))
        muks2 = es.enter_context(nc.sbuf_tensor("muks2", [128, 8], BF16))
        maskb = es.enter_context(nc.sbuf_tensor("maskb", [128, 128], BF16))
        idenb = es.enter_context(nc.sbuf_tensor("idenb", [128, 128], BF16))
        mu_d = es.enter_context(nc.sbuf_tensor("mu_d", [128, 8], F32))
        cy1 = es.enter_context(nc.sbuf_tensor("cy1", [128, 8], F32))
        cy3 = es.enter_context(nc.sbuf_tensor("cy3", [128, 8], F32))
        q_ph = es.enter_context(nc.sbuf_tensor("q_ph", [128, 2048], BF16))
        k_ph = es.enter_context(nc.sbuf_tensor("k_ph", [128, 2048], BF16))
        vg0 = es.enter_context(nc.sbuf_tensor("vg0", [128, 16 * 65], BF16))
        vg1 = es.enter_context(nc.sbuf_tensor("vg1", [128, 16 * 65], BF16))
        vgs = [vg0, vg1]
        ARENA = 41056
        slab = es.enter_context(nc.sbuf_tensor("slab", [128, ARENA],
                                               mybir.dt.uint8))
        ab = nc._sbuf_addr_for_side("left") - ARENA
        def at(name, shape, dtype, off):
            return nc.alloc_sbuf_tensor_at(name, shape, dtype, offset=ab + off)
        xr = at("xr", [128, NCH * TT], BF16, 0)
        xk = at("xk", [128, NCH * TT], BF16, 8192)
        kact = at("kact", [128, NCH * TT], BF16, 16384)
        gsl0 = at("gsl0", [128, YW], F32, 16384)
        gsl1 = at("gsl1", [128, YW], F32, 18464)
        gsl2 = at("gsl2", [128, YW], F32, 20544)
        dmx = at("dmx", [128, 7 * TT], F32, 24576)
        rb = at("rb", [128, 2 * TT], BF16, 28672)
        rsq = at("rsq", [128, 2 * TT], BF16, 30720)
        sg_w = at("sg_w", [128, TT], F32, 32768)
        krl = at("krl", [128, 2 * TT], BF16, 34816)
        ybA = at("ybA", [128, YW], F32, 38912)
        q_fm = at("q_fm", [128, NCH * TT], BF16, 0)
        k_fm = at("k_fm", [128, NCH * TT], BF16, 8192)
        Yb = at("Yb", [128, NCH * TT], BF16, 16384)
        v_tm = at("v_tm", [128, 2048], BF16, 24576)
        expp = at("expp", [128, 2 * TT], BF16, 28672)
        osn = at("osn", [128, 1024], BF16, 30720)
        rzs = at("rzs", [128, 8], F32, 32768)
        rzv = at("rzv", [128, 16], F32, 32800)
        ot = at("ot", [128, 4096], BF16, 32864)
        mmA = es.enter_context(nc.psum_tensor("mmA", [128, TT], F32))
        mmB = es.enter_context(nc.psum_tensor("mmB", [128, TT], F32))
        st0 = es.enter_context(nc.psum_tensor("st0", [128, TT], F32))
        st1 = es.enter_context(nc.psum_tensor("st1", [128, TT], F32))
        pv0 = es.enter_context(nc.psum_tensor("pv0", [128, TT], F32))
        pv1 = es.enter_context(nc.psum_tensor("pv1", [128, TT], F32))
        bcp = es.enter_context(nc.psum_tensor("bcp", [128, TT], F32))
        srw = es.enter_context(nc.psum_tensor("srw", [128, TT], F32))
        SQ = es.enter_context(nc.semaphore("SQ"))
        SG = es.enter_context(nc.semaphore("SG"))
        STN = es.enter_context(nc.semaphore("STn"))
        SV = es.enter_context(nc.semaphore("SV"))
        SS = es.enter_context(nc.semaphore("SS"))
        SP = es.enter_context(nc.semaphore("SP"))
        SR = es.enter_context(nc.semaphore("SR"))
        blk = es.enter_context(nc.Block())
        sem = {"q": SQ, "g": SG, "t": STN, "v": SV, "s": SS, "p": SP,
               "r": SR}
        cnt = {k: 0 for k in sem}
        last = {}

        def mark(instr, name, k=1):
            instr.then_inc(sem[name], k)
            cnt[name] += k
            return cnt[name]

        def wt(eng, ename, name, val):
            if val <= 0:
                return
            key = (ename, name)
            if last.get(key, -1) >= val:
                return
            eng.wait_ge(sem[name], val)
            last[key] = val

        mm = [mmA, mmB]
        stb = [st0, st1]
        pvb = [pv0, pv1]

        def wsl(w, ci, dj):
            return w[:, 1024 * ci + 128 * dj:1024 * ci + 128 * dj + 128]

        @blk.gpsimd
        def _(g):
            W_SB = {"wq": w_q, "wk": w_k, "wvp": w_vp, "wop": w_op,
                    "wrt": w_rt, "wktf": w_kt, "wvtf": w_vt}
            for nm, wsb in W_SB.items():
                for ci in range(8):
                    mark(g.dma_start(
                        wsb[:, 1024 * ci:1024 * ci + 1024],
                        bass.AP(DR[nm], ci * 128 * C, [[C, 128], [1, C]])),
                        "g", 16)
            for dst, src in [(w2q[0:1, :], d_bq.ap()), (w2k[0:1, :], d_bk.ap()),
                             (r_bvp[:], d_bvp.ap())]:
                mark(g.dma_start(dst, src), "g", 16)
            g.memset(rhs2[0:1, :], 1.0)
            g.memset(negC[:], -1.0 / 1024.0)
            g.memset(onecol[:], 1.0)
            g.memset(c_pool[:, 0:1], EPSC2)
            g.memset(c_pool[:, 1:2], LN1024)
            g.memset(vg0[:], 1.0)
            g.memset(vg1[:], 1.0)
            g.memset(r_B1f[:], 0.0)
            g.memset(r_B3f[:], 0.0)
            g.memset(q_ph[:], 0.0)
            mark(g.memset(k_ph[:], 0.0), "p", 1)
        PRO_G = cnt["g"]
        PRO_P = cnt["p"]

        @blk.sync
        def _(q):
            for dst, src in [(mu_r[:], d_mur.ap()), (mu_k[:], d_muk.ap()),
                             (bo_c[:], d_bo.ap()), (maskb[:], d_mask.ap()),
                             (idenb[:], d_iden.ap())]:
                mark(q.dma_start(dst, src), "q", 16)
            for ci in range(NCH):
                mark(q.dma_start(
                    R[:, TT * ci:TT * (ci + 1)],
                    bass.AP(xT, 128 * ci * L, [[L, 128], [1, TT]])), "q", 16)
        PRO_Q = cnt["q"]

        @blk.vector
        def _(v):
            wt(v, "v", "q", PRO_Q)
            wt(v, "v", "g", PRO_G)
            wt(v, "v", "p", PRO_P)
            v.tensor_tensor(mu_d[:], mu_r[:], mu_k[:], op=AluOpType.subtract)
            v.tensor_scalar(murs[:], mu_r[:], -1.0 / 1024.0, None,
                            op0=AluOpType.mult)
            v.tensor_scalar(murs2[:], mu_r[:], 1.0 / 1024.0, -1.0 / 1024.0,
                            op0=AluOpType.mult, op1=AluOpType.add)
            v.tensor_scalar(muks[:], mu_k[:], -1.0 / 1024.0, None,
                            op0=AluOpType.mult)
            mark(v.tensor_scalar(muks2[:], mu_k[:], 1.0 / 1024.0,
                                 -1.0 / 1024.0, op0=AluOpType.mult,
                                 op1=AluOpType.add), "v", 1)
        PRO_V = cnt["v"]

        cs_jobs = [(w2q, 1, w_q, None), (w2k, 1, w_k, None),
                   (r_csvp, 0, w_vp, None),
                   (f2r, 0, w_rt, murs), (f2r, 1, w_rt, murs2),
                   (f2k, 0, w_kt, muks), (f2k, 1, w_kt, muks2)]
        cs_marks = []
        cs_qm = {}
        cs_vm0 = {}
        cs_vm = []

        for half in range(2):
            for ji, (row, part, w, cols) in enumerate(cs_jobs):
                i = half * len(cs_jobs) + ji

                @blk.tensor
                def _(t, i=i, w=w, cols=cols, half=half):
                    if i == 0:
                        wt(t, "t", "g", PRO_G)
                        wt(t, "t", "p", PRO_P)
                        wt(t, "t", "v", PRO_V)
                    else:
                        wt(t, "t", "v", cs_vm0[i - 1])
                    for ci in range(8):
                        col = negC[:, 0:1] if cols is None else cols[:, ci:ci + 1]
                        ins = t.matmul(srw[0:1, :], col,
                                       w[:, 1024 * ci + 512 * half:
                                         1024 * ci + 512 * half + 512],
                                       start=(ci == 0), stop=(ci == 7))
                    cs_marks.append(mark(ins, "t", 1))

                @blk.vector
                def _(v, i=i, row=row, part=part, half=half):
                    wt(v, "v", "t", cs_marks[i])
                    if i >= 1:
                        wt(v, "v", "q", cs_qm.get(i - 1, 0))
                    if part == 0:
                        ins = v.tensor_copy(
                            row[0:1, 512 * half:512 * half + 512],
                            srw[0:1, :])
                    else:
                        ins = v.tensor_copy(r_csx[0:1, :], srw[0:1, :])
                    m = mark(ins, "v", 1)
                    cs_vm.append(m)
                    cs_vm0[i] = m

                @blk.sync
                def _(q, i=i, row=row, part=part, half=half):
                    if part == 1:
                        wt(q, "q", "v", cs_vm[i])
                        mark(q.dma_start(
                            row[1:2, 512 * half:512 * half + 512],
                            r_csx[0:1, :]), "q", 16)
                    cs_qm[i] = cnt["q"]

        CSROWS_V = cnt["v"]
        CSROWS_Q = cnt["q"]

        prev_store_q = 0
        ln_state = {}

        def ln_phase(it, ln_i, ready=None, rsem="p", t_gate=None):
            rows_h = [0]
            b_row = {1: B1d, 2: rhs2, 3: B3d}[ln_i]
            carry = ln_i in (1, 3)
            seq_start = (it % 4 == 0)
            rb_marks = []
            stat_marks = []

            for j in range(NCH):
                sl = slice(TT * (j % 2), TT * (j % 2 + 1))

                @blk.vector
                def _(v, j=j, sl=sl):
                    if j >= 2:
                        wt(v, "v", "t", stat_marks[j - 2])
                    if ready is not None:
                        wt(v, "v", rsem, ready[j])
                    if j == 0:
                        wt(v, "v", "t",
                           cnt["t"] if t_gate is None else t_gate)
                    v.tensor_copy(rb[:, sl], R[:, TT * j:TT * (j + 1)])
                    ins = v.tensor_tensor(rsq[:, sl], rb[:, sl], rb[:, sl],
                                          op=AluOpType.mult)
                    rb_marks.append(mark(ins, "v", 1))

                @blk.tensor
                def _(t, j=j, sl=sl):
                    wt(t, "t", "v", rb_marks[j])
                    t.matmul(srw[0:1, :], onecol[:], rb[:, sl],
                             start=(j == 0), stop=(j == NCH - 1))
                    ins = t.matmul(bcp[32:33, :], onecol[:], rsq[:, sl],
                                   start=(j == 0), stop=(j == NCH - 1))
                    stat_marks.append(mark(ins, "t", 1))

            @blk.vector
            def _(v):
                wt(v, "v", "t", stat_marks[NCH - 1])
                v.tensor_copy(r_s1[:], srw[0:1, :])
                v.tensor_tensor(r_rho[:], r_s1[:], r_s1[:], op=AluOpType.mult)
                mark(v.scalar_tensor_tensor(r_s2[:], bcp[32:33, :], 1024.0,
                                            r_rho[:], op0=AluOpType.mult,
                                            op1=AluOpType.subtract), "v", 1)
            v_u = cnt["v"]

            @blk.scalar
            def _(s):
                wt(s, "s", "v", v_u)
                s.activation(r_s2[:], r_s2[:], AF.Ln, bias=c_pool[0:1, 0:1])
                s.activation(r_rho[:], r_s2[:], AF.Exp, scale=-0.5,
                             bias=c_pool[0:1, 1:2])
                mark(s.activation(r_rhoh[:], r_rho[:], AF.Copy,
                                  bias=0.0), "s", 1)
            s_rho = cnt["s"]

            @blk.vector
            def _(v):
                wt(v, "v", "s", s_rho)
                mark(v.scalar_tensor_tensor(r_rhol[:], r_rhoh[:], -1.0,
                                            r_rho[:], op0=AluOpType.mult,
                                            op1=AluOpType.add), "v", 1)
            v_rows = cnt["v"]
            rows_h[0] = s_rho

            @blk.vector
            def _(v):
                if carry:
                    bf_row = r_B1f if ln_i == 1 else r_B3f
                    v.tensor_copy(bf_row[0:1, 0:1], bf_row[0:1, TT:TT + 1])
                    v.drain()
                    ins = v.tensor_tensor(bf_row[0:1, 1:TT + 1], r_s1[:],
                                          r_rho[:], op=AluOpType.mult)
                    if seq_start:
                        ins = v.memset(bf_row[0:1, 0:1], 0.0)
                else:
                    ins = v.tensor_tensor(r_B2L[0:1, 0:TT], r_s1[:],
                                          r_rho[:], op=AluOpType.mult)
                mark(ins, "v", 1)
            v_brow = cnt["v"]

            @blk.sync
            def _(q):
                wt(q, "q", "v", v_brow)
                if carry:
                    bf_row = r_B1f if ln_i == 1 else r_B3f
                    mark(q.dma_start(b_row[0:1, 0:TT],
                                     bf_row[0:1, 1:TT + 1]), "q", 16)
                    mark(q.dma_start(b_row[1:2, 0:TT],
                                     bf_row[0:1, 0:TT]), "q", 16)
                else:
                    mark(q.dma_start(rhs2[1:2, 0:TT],
                                     r_B2L[0:1, 0:TT]), "q", 16)
            ln_state[(it, ln_i, "bq")] = cnt["q"]

            @blk.tensor
            def _(t):
                wt(t, "t", "s", rows_h[0])
                t.matmul(bcp[:], rhs2[0:1, 0:128], r_rhoh[:],
                         start=True, stop=False)
                wt(t, "t", "v", v_rows)
                ins = t.matmul(bcp[:], rhs2[0:1, 0:128], r_rhol[:],
                               start=False, stop=True)
                mark(ins, "t", 1)
            t_bc = cnt["t"]

            if ln_i == 2:
                @blk.vector
                def _(v):
                    wt(v, "v", "t", max(t_bc, cnt["t"]))
                    for j in range(NCH):
                        ins = v.tensor_tensor(Yb[:, TT * j:TT * (j + 1)],
                                              R[:, TT * j:TT * (j + 1)],
                                              bcp[:], op=AluOpType.mult)
                        ln_state[(it, 2, "yb", j)] = mark(ins, "v", 1)
            ln_state[(it, ln_i)] = cnt["v"]
            ln_state[(it, ln_i, "bc")] = t_bc
            return b_row


        def ffn_emit(it, b_row, bq_mark, mix_k, mix_r):
            bpair = b_row[0:2, 0:TT]
            kev = {}
            kevp = {}
            fev = {}
            fsg = {}
            grm = {}
            kmm_marks = {}

            s_now = cnt["s"]
            for dj in range(NCH):
                bank = mm[dj % 2]

                @blk.tensor
                def _(t, dj=dj, bank=bank):
                    if dj < 2:
                        wt(t, "t", "s", s_now)
                        wt(t, "t", "q", bq_mark)
                    if dj >= 2:
                        wt(t, "t", "v", kev[dj - 2])
                    for oi, ci in enumerate(MIX_ORDER):
                        sm, val = mix_k[ci]
                        wt(t, "t", sm, val)
                        t.matmul(bank[:], wsl(w_kt, ci, dj),
                                 xk[:, TT * ci:TT * (ci + 1)],
                                 start=(oi == 0), stop=False)
                    ins = t.matmul(bank[:],
                                   f2k[0:2, 128 * dj:128 * dj + 128],
                                   bpair, start=False, stop=True)
                    kmm_marks[dj] = mark(ins, "t", 1)

                @blk.vector
                def _(v, dj=dj, bank=bank):
                    wt(v, "v", "t", kmm_marks[dj])
                    wt(v, "v", "p", kevp.get(dj - 2, 0))
                    sl = krl[:, TT * (dj % 2):TT * (dj % 2 + 1)]
                    ins = v.tensor_scalar(sl, bank[:], 0.0, None,
                                          op0=AluOpType.max)
                    kev[dj] = mark(ins, "v", 1)

                @blk.gpsimd
                def _(g, dj=dj):
                    wt(g, "g", "v", kev[dj])
                    sl = krl[:, TT * (dj % 2):TT * (dj % 2 + 1)]
                    ins = g.tensor_tensor(kact[:, TT * dj:TT * (dj + 1)],
                                          sl, sl, op=AluOpType.mult)
                    kevp[dj] = mark(ins, "p", 1)

            rpre_m = {}
            vpre_m = {}
            for dj in range(NCH):
                bankR = mm[dj % 2]
                bankV = stb[dj % 2]

                @blk.tensor
                def _(t, dj=dj, bankR=bankR, bankV=bankV):
                    if dj < 2:
                        wt(t, "t", "s", s_now)
                    if dj >= 2:
                        wt(t, "t", "s", fsg[dj - 2])
                        wt(t, "t", "v", fev[dj - 2])
                    for oi, ci in enumerate(MIX_ORDER):
                        sm, val = mix_r[ci]
                        wt(t, "t", sm, val)
                        t.matmul(bankR[:], wsl(w_rt, ci, dj),
                                 xr[:, TT * ci:TT * (ci + 1)],
                                 start=(oi == 0), stop=False)
                    ins = t.matmul(bankR[:],
                                   f2r[0:2, 128 * dj:128 * dj + 128],
                                   bpair, start=False, stop=True)
                    rpre_m[dj] = mark(ins, "t", 1)
                    for ci in range(NCH):
                        wt(t, "t", "p", kevp[ci])
                        ins = t.matmul(bankV[:], wsl(w_vt, ci, dj),
                                       kact[:, TT * ci:TT * (ci + 1)],
                                       start=(ci == 0), stop=(ci == NCH - 1))
                    vpre_m[dj] = mark(ins, "t", 1)

                @blk.scalar
                def _(s, dj=dj, bankR=bankR):
                    wt(s, "s", "t", rpre_m[dj])
                    wt(s, "s", "v", fev.get(dj - 1, 0))
                    ins = s.activation(sg_w[:], bankR[:], AF.Sigmoid)
                    fsg[dj] = mark(ins, "s", 1)

                @blk.vector
                def _(v, dj=dj, bankV=bankV):
                    wt(v, "v", "s", fsg[dj])
                    wt(v, "v", "t", vpre_m[dj])
                    wt(v, "v", "p", grm.get(dj - 1, 0))
                    ins = v.scalar_tensor_tensor(sg_w[:], sg_w[:], 0.5,
                                                 bankV[:],
                                                 op0=AluOpType.mult,
                                                 op1=AluOpType.mult)
                    fev[dj] = mark(ins, "v", 1)

                @blk.gpsimd
                def _(g, dj=dj):
                    wt(g, "g", "v", fev[dj])
                    ins = g.tensor_tensor(R[:, TT * dj:TT * (dj + 1)],
                                           R[:, TT * dj:TT * (dj + 1)],
                                           sg_w[:], op=AluOpType.add)
                    grm[dj] = mark(ins, "p", 1)
            return grm

        MIX_ORDER = list(range(NCH))

        def mix_emit(it, ln_i, b_row):
            mix_k, mix_r = {}, {}
            y_ready = ln_state[(it, ln_i)]
            t_bc = ln_state[(it, ln_i, "bc")]
            seq_start = (it % 4 == 0)
            cyb = cy1 if ln_i == 1 else cy3
            slots = [gsl0, gsl1, gsl2, ybA]

            slot_m = {}
            d_m = {}

            def S(j):
                slot = slots[j % 4]

                @blk.vector
                def _(v, j=j, slot=slot):
                    if j == 0:
                        wt(v, "v", "v", y_ready)
                        wt(v, "v", "t", max(t_bc, cnt["t"]))
                    if seq_start:
                        v.memset(slot[:, 0:1], 0.0)
                    else:
                        v.tensor_copy(slot[:, 0:1], cyb[:, j:j + 1])
                    v.tensor_tensor(slot[:, 1:YW], R[:, TT * j:TT * (j + 1)],
                                    bcp[:], op=AluOpType.mult)
                    v.drain()
                    ins = v.tensor_copy(cyb[:, j:j + 1], slot[:, TT:TT + 1])
                    slot_m[j] = mark(ins, "v", 1)

                @blk.gpsimd
                def _(g, j=j, slot=slot):
                    wt(g, "g", "v", slot_m[j])
                    dsl = dmx[:, TT * (j % 7):TT * (j % 7 + 1)]
                    ins = g.tensor_tensor(dsl, slot[:, 1:YW],
                                          slot[:, 0:TT], op=AluOpType.subtract)
                    d_m[j] = mark(ins, "p", 1)

            def XK(j):
                slot = slots[j % 4]

                @blk.vector
                def _(v, j=j, slot=slot):
                    wt(v, "v", "p", d_m[j])
                    dsl = dmx[:, TT * (j % 7):TT * (j % 7 + 1)]
                    ins = v.scalar_tensor_tensor(
                        xk[:, TT * j:TT * (j + 1)], dsl,
                        mu_k[:, j:j + 1], slot[:, 0:TT],
                        op0=AluOpType.mult, op1=AluOpType.add)
                    mix_k[j] = ("v", mark(ins, "v", 1))

            def XR(j):
                @blk.vector
                def _(v, j=j):
                    dsl = dmx[:, TT * (j % 7):TT * (j % 7 + 1)]
                    ins = v.scalar_tensor_tensor(
                        xr[:, TT * j:TT * (j + 1)], dsl,
                        mu_d[:, j:j + 1], xk[:, TT * j:TT * (j + 1)],
                        op0=AluOpType.mult, op1=AluOpType.add)
                    mix_r[j] = ("v", mark(ins, "v", 1))

            S(0)
            S(1)
            for j in range(NCH):
                XK(j)
                if j == 5:
                    XR(0)
                if j + 2 < NCH:
                    S(j + 2)
            for j in range(1, NCH):
                XR(j)

            return mix_k, mix_r

        def attn_emit(it):
            prev = attn_prev
            qk_ev = {}
            v_now0 = cnt["v"]
            s_now0 = cnt["s"]
            jobs = [(w_q, q_fm, w2q, dj) for dj in range(NCH)] + \
                   [(w_k, k_fm, w2k, dj) for dj in range(NCH)]
            for i, (wmat, dstb, pair, dj) in enumerate(jobs):
                bank = mm[i % 2]

                @blk.tensor
                def _(t, i=i, wmat=wmat, dj=dj, bank=bank, pair=pair):
                    if i < 2:
                        wt(t, "t", "s", s_now0)
                        wt(t, "t", "q", ln_state[(it, 2, "bq")])
                    else:
                        wt(t, "t", "v", qk_ev[i - 2])
                    for ci in range(NCH):
                        wt(t, "t", "v", ln_state[(it, 2, "yb", ci)])
                        t.matmul(bank[:], wsl(wmat, ci, dj),
                                 Yb[:, TT * ci:TT * (ci + 1)],
                                 start=(ci == 0), stop=False)
                    ins = t.matmul(bank[:],
                                   pair[0:2, 128 * dj:128 * dj + 128],
                                   rhs2[0:2, 0:TT], start=False, stop=True)
                    qk_ev[(i, "t")] = mark(ins, "t", 1)

                @blk.vector
                def _(v, i=i, dj=dj, bank=bank, dstb=dstb):
                    wt(v, "v", "t", qk_ev[(i, "t")])
                    dst = bass.AP(dstb, 8 * dj,
                                  [[NCH * TT, 128], [1024, 4], [1, 8],
                                   [64, 16]])
                    srcp = bass.AP(bank, 0,
                                   [[TT, 128], [128, 4], [16, 8], [1, 16]])
                    ins = v.tensor_copy(dst, srcp)
                    qk_ev[i] = mark(ins, "v", 1)

            qk_done = cnt["v"]

            sub_state = {}

            vt_ms = {}

            def vpathA(s_i):
                vt_m = vt_ms.setdefault(s_i, {})
                vsl = v_tm[:, 1024 * (s_i % 2):1024 * (s_i % 2) + 1024]
                for nh in range(2):
                    bank = mm[nh]

                    @blk.tensor
                    def _(t, s_i=s_i, nh=nh, bank=bank):
                        wt(t, "t", "v", qk_done)
                        if s_i > 0:
                            wt(t, "t", "v", sub_state[(s_i - 1, "vtm_ev")])
                        for ci in range(NCH):
                            wt(t, "t", "v", ln_state[(it, 2, "yb", ci)])
                            t.matmul(bank[:],
                                     Yb[:, TT * ci + 128 * s_i:
                                        TT * ci + 128 * s_i + 128],
                                     w_vp[:, 1024 * ci + 512 * nh:
                                          1024 * ci + 512 * nh + 512],
                                     start=(ci == 0), stop=False)
                        t.matmul(bank[:], rhs2[0:1, 0:128],
                                 r_bvp[0:1, 512 * nh:512 * nh + 512],
                                 start=False, stop=False)
                        ins = t.matmul(
                            bank[:], r_B2L[0:1, 128 * s_i:128 * s_i + 128],
                            r_csvp[0:1, 512 * nh:512 * nh + 512],
                            start=False, stop=True)
                        vt_m[nh] = mark(ins, "t", 1)

            def vpathAe(s_i):
                vt_m = vt_ms[s_i]
                vsl = v_tm[:, 1024 * (s_i % 2):1024 * (s_i % 2) + 1024]

                @blk.vector
                def _(v, s_i=s_i, vsl=vsl, vt_m=vt_m):
                    wt(v, "v", "t", vt_m[1])
                    pk = (s_i - 2, "vgd_r")
                    if pk in sub_state:
                        wt(v, "v", "r", sub_state[pk])
                        wt(v, "v", "q", sub_state[(s_i - 2, "vgd_q")])
                    else:
                        pr = prev.get("vgd%d" % (s_i % 2))
                        if pr is not None:
                            wt(v, "v", "r", pr[0])
                            wt(v, "v", "q", pr[1])
                    v.tensor_copy(vsl[:, 0:512], mm[0][:])
                    ins = v.tensor_copy(vsl[:, 512:1024], mm[1][:])
                    sub_state[(s_i, "vtm_ev")] = mark(ins, "v", 1)

            def vpathB(s_i):
                vsl = v_tm[:, 1024 * (s_i % 2):1024 * (s_i % 2) + 1024]
                vgb = vgs[s_i % 2]

                @blk.scalar
                def _(s, s_i=s_i, vsl=vsl, vgb=vgb):
                    wt(s, "s", "v", sub_state[(s_i, "vtm_ev")])
                    wt(s, "s", "t", sub_state.get((s_i - 2, "pv_all"),
                                                  prev.get("pv%d" % (s_i % 2), 0)))
                    for sg in range(0, 16, 2):
                        srcap = bass.AP(vsl.tensor if hasattr(vsl, "tensor")
                                        else v_tm,
                                        1024 * (s_i % 2) + 64 * sg,
                                        [[2048, 128], [1, 64]])
                        dstap = bass.AP(vgb, 8 * sg * (16 * 65),
                                        [[16 * 65, 8], [65, 16], [1, 64]])
                        mark(s.dma_start(dstap, srcap), "r", 16)
                    sub_state[(s_i, "vgd_r")] = cnt["r"]

                @blk.sync
                def _(q, s_i=s_i, vsl=vsl, vgb=vgb):
                    wt(q, "q", "v", sub_state[(s_i, "vtm_ev")])
                    wt(q, "q", "t", sub_state.get((s_i - 2, "pv_all"),
                                                  prev.get("pv%d" % (s_i % 2), 0)))
                    for sg in range(1, 16, 2):
                        srcap = bass.AP(vsl.tensor if hasattr(vsl, "tensor")
                                        else v_tm,
                                        1024 * (s_i % 2) + 64 * sg,
                                        [[2048, 128], [1, 64]])
                        dstap = bass.AP(vgb, 8 * sg * (16 * 65),
                                        [[16 * 65, 8], [65, 16], [1, 64]])
                        mark(q.dma_start(dstap, srcap), "q", 16)
                    sub_state[(s_i, "vgd_q")] = cnt["q"]

            def parity(s_i):
                @blk.sync
                def _(q, s_i=s_i):
                    if s_i > 0:
                        wt(q, "q", "t", sub_state[(s_i - 1, "sc_all")])
                    else:
                        wt(q, "q", "t", prev.get("sc_all", 0))
                    wt(q, "q", "v", qk_done)
                    for par in range(2):
                        for srcb, dstb in [(q_fm, q_ph), (k_fm, k_ph)]:
                            srcap = bass.AP(srcb,
                                            64 * par * NCH * TT + 1024 * s_i,
                                            [[NCH * TT, 64], [64, 16], [1, 64]])
                            dstap = bass.AP(dstb, 64 * par,
                                            [[2048, 64], [128, 16], [1, 64]])
                            mark(q.dma_start(dstap, srcap), "q", 16)
                    sub_state[(s_i, "phd")] = cnt["q"]

            vpathA(0)
            vpathAe(0)
            parity(0)
            vpathB(0)
            for s_i in range(4):
                vgb = vgs[s_i % 2]
                if s_i < 3:
                    vpathA(s_i + 1)

                sc_m = {}
                ex_m = {}
                pv_m = {}

                def emit_sc(bnk, s_i=s_i):
                    bank = stb[bnk % 2]

                    @blk.tensor
                    def _(t, s_i=s_i, bnk=bnk, bank=bank):
                        wt(t, "t", "q", sub_state[(s_i, "phd")])
                        if bnk >= 2:
                            wt(t, "t", "s", ex_m[bnk - 2])
                        elif s_i > 0:
                            wt(t, "t", "s", sub_state[(s_i - 1, "ex", bnk + 2)])
                        for gl in range(4):
                            g = 4 * bnk + gl
                            t.matmul(bank[:, 128 * gl:128 * gl + 128],
                                     k_ph[0:64, 128 * g:128 * g + 128],
                                     q_ph[0:64, 128 * g:128 * g + 128],
                                     start=(gl == 0), stop=False)
                        mrhs = bass.AP(maskb, 0, [[128, 128], [0, 4],
                                                  [1, 128]])
                        ins = t.matmul(bank[:], idenb[:], mrhs,
                                       start=False, stop=True)
                        sc_m[bnk] = mark(ins, "t", 1)

                def emit_exp(bnk, s_i=s_i):
                    bank = stb[bnk % 2]

                    @blk.scalar
                    def _(s, s_i=s_i, bnk=bnk, bank=bank):
                        wt(s, "s", "t", sc_m[bnk])
                        if bnk >= 2:
                            wt(s, "s", "t", pv_m[bnk - 2])
                        elif s_i > 0:
                            wt(s, "s", "t", sub_state[(s_i - 1, "pv", bnk + 2)])
                        ins = s.activation(
                            expp[:, TT * (bnk % 2):TT * (bnk % 2 + 1)],
                            bank[:], AF.Exp, scale=0.125)
                        ex_m[bnk] = mark(ins, "s", 1)

                def emit_pv(bnk, s_i=s_i, vgb=vgb):
                    @blk.tensor
                    def _(t, s_i=s_i, bnk=bnk, vgb=vgb):
                        wt(t, "t", "s", ex_m[bnk])
                        wt(t, "t", "r", sub_state[(s_i, "vgd_r")])
                        wt(t, "t", "q", sub_state[(s_i, "vgd_q")])
                        pbank = pvb[bnk % 2]
                        if bnk >= 2:
                            wt(t, "t", "v", pv_m.get((bnk - 2, "rz"), 0))
                        elif s_i > 0:
                            wt(t, "t", "v",
                               sub_state[(s_i - 1, "rz", bnk + 2)])
                        for gl in range(4):
                            g = 4 * bnk + gl
                            ins = t.matmul(pbank[:, 65 * gl:65 * gl + 65],
                                           expp[:, TT * (bnk % 2) + 128 * gl:
                                                TT * (bnk % 2) + 128 * gl + 128],
                                           vgb[:, 65 * g:65 * g + 65],
                                           start=True, stop=True)
                        pv_m[bnk] = mark(ins, "t", 1)

                    @blk.vector
                    def _(v, s_i=s_i, bnk=bnk):
                        wt(v, "v", "t", pv_m[bnk])
                        pbank = pvb[bnk % 2]
                        v.tensor_copy(rzs[:, 0:4],
                                      bass.AP(pbank, 64, [[TT, 128], [65, 4]]))
                        v.drain()
                        v.reciprocal(rzv[:, 4 * bnk:4 * bnk + 4], rzs[:, 0:4])
                        v.drain()
                        rzb = bass.AP(rzv, 4 * bnk,
                                      [[16, 128], [1, 4], [0, 64]])
                        ins = v.tensor_tensor(
                            bass.AP(osn, 256 * bnk, [[1024, 128], [1, 256]]),
                            bass.AP(pbank, 0, [[TT, 128], [65, 4], [1, 64]]),
                            rzb, op=AluOpType.mult)
                        pv_m[(bnk, "rz")] = mark(ins, "v", 1)

                emit_sc(0)
                emit_exp(0)
                emit_sc(1)
                emit_pv(0)
                emit_exp(1)
                emit_sc(2)
                emit_pv(1)
                emit_exp(2)
                emit_sc(3)
                emit_pv(2)
                emit_exp(3)
                emit_pv(3)
                sub_state[(s_i, "sc_all")] = sc_m[3]
                if s_i < 3:
                    vpathAe(s_i + 1)
                    parity(s_i + 1)
                    vpathB(s_i + 1)

                @blk.vector
                def _(v, s_i=s_i):
                    wt(v, "v", "v", pv_m[(3, "rz")])
                    v.drain()
                    ins = v.transpose(ot[:, 1024 * s_i:1024 * s_i + 1024],
                                      osn[:])
                    sub_state[(s_i, "ot")] = mark(ins, "v", 1)
                sub_state[(s_i, "pv_all")] = pv_m[3]
                for bnk in range(4):
                    sub_state[(s_i, "ex", bnk)] = ex_m[bnk]
                    sub_state[(s_i, "pv", bnk)] = pv_m[bnk]
                    sub_state[(s_i, "rz", bnk)] = pv_m[(bnk, "rz")]

            wo_m = {}
            grm = {}
            for dj in range(NCH):
                bank = mm[dj % 2]

                @blk.tensor
                def _(t, dj=dj, bank=bank):
                    wt(t, "t", "v", sub_state[(3, "ot")])
                    if dj >= 2:
                        wt(t, "t", "v", wo_m.get((dj - 2, "ev"), 0))
                    for f in range(8):
                        off = 32 * (f // 4) + 8 * (f % 4)
                        rhs = bass.AP(ot, off, [[4096, 128], [64, 64], [1, 8]])
                        ins = t.matmul(bank[:], wsl(w_op, f, dj), rhs,
                                       start=(f == 0), stop=(f == 7))
                    wo_m[dj] = mark(ins, "t", 1)

                @blk.vector
                def _(v, dj=dj, bank=bank):
                    wt(v, "v", "t", wo_m[dj])
                    for s_i in range(4):
                        dst = bass.AP(R, TT * dj + 128 * s_i,
                                      [[NCH * TT, 128], [1, 16], [16, 8]])
                        src = bass.AP(bank, 128 * s_i,
                                      [[TT, 128], [8, 16], [1, 8]])
                        ins = v.scalar_tensor_tensor(
                            dst, src, bo_c[:, dj:dj + 1], dst,
                            op0=AluOpType.add, op1=AluOpType.add)
                    wo_m[(dj, "ev")] = mark(ins, "v", 1)
            attn_prev.clear()
            attn_prev.update({"vgd0": (sub_state[(2, "vgd_r")],
                                       sub_state[(2, "vgd_q")]),
                              "vgd1": (sub_state[(3, "vgd_r")],
                                       sub_state[(3, "vgd_q")]),
                              "sc_all": sub_state[(3, "sc_all")],
                              "pv0": sub_state[(2, "pv_all")],
                              "pv1": sub_state[(3, "pv_all")]})
            return {dj: wo_m[(dj, "ev")] for dj in range(NCH)}

        store_marks = []
        attn_prev = {}
        prev_attn_t = [0]
        for it in range(NTILE):
            seq, t0 = it // 4, (it % 4) * TT

            xl_marks = []
            if it > 0:
                @blk.sync
                def _(q, it=it, seq=seq, t0=t0):
                    for ci in range(NCH):
                        wt(q, "q", "q", store_marks[-1][ci])
                        mark(q.dma_start(
                            R[:, TT * ci:TT * (ci + 1)],
                            bass.AP(xT, seq * C * L + 128 * ci * L + t0,
                                    [[L, 128], [1, TT]])), "q", 16)
                        xl_marks.append(cnt["q"])
            else:
                xl_marks = [PRO_Q] * NCH

            if it == 0:
                @blk.vector
                def _(v):
                    wt(v, "v", "v", CSROWS_V)
                    wt(v, "v", "q", PRO_Q)
                    mark(v.nop(), "v", 1)

            ln_phase(it, 1, ready=xl_marks, rsem="q",
                     t_gate=prev_attn_t[0] if it > 0 else None)
            mk1, mr1 = mix_emit(it, 1, B1d)
            grm1 = ffn_emit(it, B1d, ln_state[(it, 1, 'bq')], mk1, mr1)

            ln_phase(it, 2, ready=grm1)
            wo_ev = attn_emit(it)
            prev_attn_t[0] = cnt["t"]

            ln_phase(it, 3, t_gate=prev_attn_t[0])
            mk3, mr3 = mix_emit(it, 3, B3d)
            grm = ffn_emit(it, B3d, ln_state[(it, 3, 'bq')], mk3, mr3)

            @blk.sync
            def _(q, it=it, seq=seq, t0=t0):
                sm = []
                for ci in range(NCH):
                    wt(q, "q", "p", grm[ci])
                    mark(q.dma_start(
                        bass.AP(outT, seq * C * L + 128 * ci * L + t0,
                                [[L, 128], [1, TT]]),
                        R[:, TT * ci:TT * (ci + 1)]), "q", 16)
                    sm.append(cnt["q"])
                store_marks.append(sm)

        @blk.sync
        def _(q):
            wt(q, "q", "q", cnt["q"])

        @blk.gpsimd
        def _(g):
            wt(g, "g", "q", cnt["q"])

    return nc


_sigma_head = [2 * (s % 8) + s // 8 for s in range(16)]
_PERMV = np.array([64 * _sigma_head[s] + d for s in range(16)
                   for d in range(64)])


def _wo_perm(Wo):
    WoP = np.zeros_like(Wo)
    out = np.zeros((8, 128, 1024), dtype=Wo.dtype)
    for f in range(8):
        dhi, sm4 = f // 4, f % 4
        for p in range(128):
            sq, dlo = p // 32, p % 32
            sigma = 4 * sq + sm4
            h = 2 * (sigma % 8) + sigma // 8
            d = 32 * dhi + dlo
            out[f, p, :] = Wo[64 * h + d, :]
    return out.reshape(8 * 128, 1024)


_NC_CACHE = {}


def kernel(L0_attn_x, L0_attn_Wq, L0_attn_bq, L0_attn_Wk, L0_attn_bk,
           L0_attn_Wv, L0_attn_bv, L0_attn_Wo, L0_attn_bo,
           L0_ffn_W_r, L0_ffn_W_k, L0_ffn_W_v, L0_ffn_mu_r, L0_ffn_mu_k):
    x = np.ascontiguousarray(np.asarray(L0_attn_x, dtype=np.float32))
    Wq = np.asarray(L0_attn_Wq, np.float32)
    Wk = np.asarray(L0_attn_Wk, np.float32)
    Wv = np.asarray(L0_attn_Wv, np.float32)
    Wo = np.asarray(L0_attn_Wo, np.float32)
    Wr_f = np.asarray(L0_ffn_W_r, np.float32)
    Wk_f = np.asarray(L0_ffn_W_k, np.float32)
    Wv_f = np.asarray(L0_ffn_W_v, np.float32)

    mask = np.full((128, 128), -30000.0, dtype=np.float32)
    for te in range(8):
        for se in range(16):
            for sh in range(16):
                mask[8 * se + te, 8 * sh + te] = 0.0
    iden = np.eye(128, dtype=np.float32)

    common = {
        "wq": np.ascontiguousarray(Wq),
        "wk": np.ascontiguousarray(Wk),
        "wvp": np.ascontiguousarray(Wv[:, _PERMV]),
        "wop": np.ascontiguousarray(_wo_perm(Wo)),
        "wrt": np.ascontiguousarray(Wr_f.T),
        "wktf": np.ascontiguousarray(Wk_f.T),
        "wvtf": np.ascontiguousarray(Wv_f.T),
        "bq": np.asarray(L0_attn_bq, np.float32).reshape(1, C),
        "bk": np.asarray(L0_attn_bk, np.float32).reshape(1, C),
        "bvp": np.asarray(L0_attn_bv, np.float32)[_PERMV].reshape(1, C),
        "bo": np.ascontiguousarray(
            np.asarray(L0_attn_bo, np.float32).reshape(8, 128).T),
        "mur": np.ascontiguousarray(
            np.asarray(L0_ffn_mu_r, np.float32).reshape(8, 128).T),
        "muk": np.ascontiguousarray(
            np.asarray(L0_ffn_mu_k, np.float32).reshape(8, 128).T),
        "maskc": mask.astype(bft),
        "idenc": iden.astype(bft),
    }

    xT = np.ascontiguousarray(x.transpose(0, 2, 1))
    in_maps = []
    for core in range(NCORES):
        m = dict(common)
        m["xT"] = np.ascontiguousarray(
            xT[SEQN * core:SEQN * (core + 1)].reshape(SEQN * C, L))
        in_maps.append(m)

    if "nc" not in _NC_CACHE:
        _NC_CACHE["nc"] = build_nc()
    res = run_bass_kernel_spmd(_NC_CACHE["nc"], in_maps,
                               core_ids=list(range(NCORES)))
    out = np.empty((B, L, C), dtype=np.float32)
    for core in range(NCORES):
        o = res.results[core]["outT"].reshape(SEQN, C, L)
        out[SEQN * core:SEQN * (core + 1)] = o.transpose(0, 2, 1)
    return out
